# version 18
# baseline (speedup 1.0000x reference)
import numpy as np
import ml_dtypes
import concourse.bass as bass
import concourse.mybir as mybir
from concourse.bass_utils import run_bass_kernel_spmd

F32 = mybir.dt.float32
BF16 = mybir.dt.bfloat16
AF = mybir.ActivationFunctionType
ALU = mybir.AluOpType

D = 1024
H = 4096
SEQ = 2048
BATCH = 16
NCORES = 8
BLOC = BATCH // NCORES
TOK = BLOC * SEQ
T = 512
NT = TOK // T
TPS = SEQ // T
TI = T // 128
DC = D // 128
HCN = H // 128
EPS = 1e-5
INV_D = 1.0 / D

WSHAPES = {
    "B": (D, D), "W": (D, D),
    "w1": (H, D), "w2": (H, D),
    "h1": (H, H), "h2": (H, H),
    "w3": (D, H),
}
WSRC = {
    "B": "L0_attn_B", "W": "L0_attn_W",
    "w1": "L0_ffn_w1_w", "w2": "L0_ffn_w2_w",
    "h1": "L0_ffn_h1_w", "h2": "L0_ffn_h2_w",
    "w3": "L0_ffn_w3_w",
}

VA, VB, VS3, VS1, VS2, VH1, VH2 = 0, 8, 16, 24, 56, 88, 120
NVEC = 152

SSUM, SNEG, SSQD, SVEPS, SSTD, SRSTD = 0, 8, 16, 24, 32, 40
NSTAT = 48

PE, V, S, SP, G = "pe", "v", "s", "sp", "g"


def build():
    nc = bass.Bass("TRN2", target_bir_lowering=False)

    xin = nc.dram_tensor("xin", [TOK, D], F32, kind="ExternalInput").ap()
    out = nc.dram_tensor("out", [TOK, D], F32, kind="ExternalOutput").ap()
    w_in = {n: nc.dram_tensor("w_" + n, list(sh), F32, kind="ExternalInput").ap()
            for n, sh in WSHAPES.items()}
    vecs_in = nc.dram_tensor("vecs", [128, NVEC], F32, kind="ExternalInput").ap()
    role_in = nc.dram_tensor("role", [1, 1], mybir.dt.int32, kind="ExternalInput").ap()
    idf_in = nc.dram_tensor("identf", [128, 128], F32, kind="ExternalInput").ap()
    idb_in = nc.dram_tensor("identb", [128, 128], BF16, kind="ExternalInput").ap()

    w_plain = {n: nc.dram_tensor("p_" + n, list(sh), BF16, kind="Internal",
                                 addr_space="Shared").ap()
               for n, sh in WSHAPES.items()}
    w_t = {n: nc.dram_tensor("t_" + n, [sh[1], sh[0]], BF16,
                             kind="Internal").ap()
           for n, sh in WSHAPES.items()}

    def wt_block(n, k0, k1, o0, o1):
        return (w_t[n][k0:k1, o0:o1]
                .rearrange("(kc p) o -> p kc o", p=128))
    BARGROUPS = [("B", "W"), ("w1", "w2"), ("h1",), ("h2",), ("w3",)]
    BAR_OF = {n: i for i, grp in enumerate(BARGROUPS) for n in grp}
    bar_bufs = [(nc.dram_tensor(f"bar_i{i}", [4, 4], F32, kind="Internal").ap(),
                 nc.dram_tensor(f"bar_o{i}", [4, 4], F32, kind="Internal").ap())
                for i in range(len(BARGROUPS))]

    from contextlib import ExitStack
    with ExitStack() as ctx:
        def sb(name, shape, dt):
            return ctx.enter_context(nc.sbuf_tensor(name, shape, dt))

        def ps(name, shape, dt):
            return ctx.enter_context(nc.psum_tensor(name, shape, dt))

        y2_tm = sb("y2_tm", [128, 2, TI, D], F32)
        ln_tm = sb("ln_tm", [128, TI, D], BF16)
        ln_fm = sb("ln_fm", [128, DC, T], BF16)
        h_fm = sb("h_fm", [128, DC, T], BF16)
        af_fm = sb("af_fm", [128, DC, T], F32)
        af_tm = sb("af_tm", [128, TI, D], F32)
        g_fm = sb("g_fm", [128, HCN, T], BF16)
        g2_fm = sb("g2_fm", [128, HCN, T], BF16)
        dump = sb("dump", [128, D], F32)
        tmp_tm = sb("tmp_tm", [128, D], F32)
        stats = sb("stats", [128, NSTAT], F32)
        abc = sb("abc", [128, 4, T], F32)
        silu = sb("silu", [128, 2, T], F32)
        carry = sb("carry", [128, 8 * DC], F32)
        ones = sb("ones", [128, T], F32)
        vecs = sb("vecs_sb", [128, NVEC], F32)
        idf = sb("idf", [128, 128], F32)
        idb = sb("idb", [128, 128], BF16)
        wstr = sb("wstr", [128, 4, 8, 512], BF16)

        ps_mm = ps("ps_mm", [128, 6, 512], F32)
        ps_trA = ps("ps_trA", [128, 1024], BF16)
        ps_trB = ps("ps_trB", [128, 512], F32)

        sem = {k: ctx.enter_context(nc.semaphore("sem_" + k))
               for k in ["LD", "LX", "LW", "PE", "SV", "SS"]}
        ccsem = [ctx.enter_context(nc.semaphore(f"sem_CC{i}"))
                 for i in range(len(BARGROUPS))]
        srole = sb("srole", [1, 2], mybir.dt.int32)
        REG = {}
        cnt = {k: 0 for k in sem}

        ops = {q: [] for q in (PE, V, S, SP, G)}
        ENG_SEM = {PE: "PE", V: "SV", S: "SS", G: "LX"}

        def emit(q, f):
            ops[q].append(f)

        def wait(q, semname, val):
            if val <= 0:
                return
            s_h = sem[semname]
            emit(q, lambda e, s_h=s_h, val=val: e.wait_ge(s_h, val))

        def dma(q, dst, src, sem_override=None, **kw):
            semname = sem_override or ("LD" if q == SP else "LX")
            cnt[semname] += 16
            tgt = cnt[semname]
            s_h = sem[semname]
            emit(q, lambda e, dst=dst, src=src, s_h=s_h, kw=kw:
                 e.dma_start(dst, src, **kw).then_inc(s_h, 16))
            return tgt

        def step(q, fn, inc=False):
            if inc:
                semname = ENG_SEM[q]
                cnt[semname] += 1
                tgt = cnt[semname]
                s_h = sem[semname]
                emit(q, lambda e, fn=fn, s_h=s_h: fn(e).then_inc(s_h, 1))
                return tgt
            emit(q, lambda e, fn=fn: fn(e))
            return None

        last_dump = [0]
        last_tmp = [0]

        rel = {}

        def use(q, name):
            r = rel.get(name)
            if r:
                wait(q, r[0], r[1])

        def release(name, semname, val):
            rel[name] = (semname, val)

        dma(G, srole[0:1, 0:1], role_in)
        dma(G, vecs[:, :], vecs_in)
        dma(G, idf[:, :], idf_in)
        dma(G, idb[:, :], idb_in)
        m_consts = cnt["LX"]
        m_ones = step(V, lambda e: e.memset(ones[:, :], 1.0), inc=True)
        wait(S, "SV", m_ones)
        wait(PE, "LX", m_consts)
        wait(S, "LX", m_consts)
        wait(V, "LX", m_consts)

        m_x0 = dma(G, y2_tm[:, 0, :, :],
                   xin[0:T, :].rearrange("(ti p) d -> p ti d", p=128))

        m_plain = {}
        wait(G, "LX", 16)
        SNAPV = {}

        def snap_offsets(e):
            r = REG[G]
            for oh in (512, 1024, 2048, 4096):
                e.reg_load(r, srole[0:1, 0:1])
                e.reg_mul(r, r, oh)
                SNAPV[oh] = e.snap(r, min_val=0, max_val=oh)
            return None
        emit(G, lambda e: snap_offsets(e))
        for gi, grp in enumerate(BARGROUPS):
            for n in grp:
                O, K = WSHAPES[n]
                OH = O // 2

                def cast_half(e, n=n, K=K, OH=OH):
                    return e.dma_start(
                        w_plain[n][bass.ds(SNAPV[OH], OH), 0:K],
                        w_in[n][bass.ds(SNAPV[OH], OH), 0:K])
                cnt["LX"] += 16
                m_plain[n] = cnt["LX"]
                s_h = sem["LX"]
                emit(G, lambda e, f=cast_half, s_h=s_h: f(e).then_inc(s_h, 16))
            wait(G, "LX", max(m_plain[n] for n in grp))
            bin_, bout_ = bar_bufs[gi]
            cc = ccsem[gi]
            emit(G, lambda e, bin_=bin_, bout_=bout_, cc=cc:
                 e.collective_compute(
                     "AllReduce", ALU.add, [list(range(NCORES))],
                     [bin_.opt()], [bout_.opt()]).then_inc(cc))

        slot_rr = [0]

        def acquire_full(q):
            s_i = slot_rr[0] % 4
            slot_rr[0] += 1
            use(q, f"half{2 * s_i}")
            use(q, f"half{2 * s_i + 1}")
            return s_i

        def release_full(s_i, semname, val):
            release(f"half{2 * s_i}", semname, val)
            release(f"half{2 * s_i + 1}", semname, val)

        half_rr = [0]

        def acquire_half(q):
            h_i = half_rr[0] % 8
            half_rr[0] += 1
            use(q, f"half{h_i}")
            return h_i

        def half_view(h_i):
            s_i, j = divmod(h_i, 2)
            return wstr[:, s_i, :, j * 256:(j + 1) * 256]

        stage2_done = {}
        bars_emitted = set()

        def flush_stage2(names):
            for n in names:
                if n in stage2_done:
                    continue
                O, K = WSHAPES[n]
                cc = ccsem[BAR_OF[n]]
                emit(SP, lambda e, cc=cc: e.wait_ge(cc, 1))
                for kc in range(K // 128):
                    s_i = acquire_full(SP)
                    tb = wstr[:, s_i, :, :].rearrange("p a b -> p (a b)")[:, 0:O]
                    m1 = dma(SP, tb, w_plain[n][0:O, kc * 128:(kc + 1) * 128],
                             transpose=True)
                    wait(G, "LD", m1)
                    m2 = dma(G, w_t[n][kc * 128:(kc + 1) * 128, 0:O], tb,
                             sem_override="LW")
                    release_full(s_i, "LW", m2)
                stage2_done[n] = cnt["LW"]
        def wait_cache(n):
            wait(SP, "LW", stage2_done[n])


        def ln_stats(src_tm, wait_list):
            for q, semname, val in wait_list:
                wait(q, semname, val)
            for ti in range(TI):
                wait(S, "SS", last_dump[0])
                m = step(S, lambda e, ti=ti: e.activation(
                    dump[:, :], src_tm[:, ti, :], AF.Copy,
                    accum_out=stats[:, SSUM + ti:SSUM + ti + 1]), inc=True)
                last_dump[0] = m
            wait(V, "SS", m)
            m_neg = step(V, lambda e: e.tensor_scalar(
                stats[:, SNEG:SNEG + TI], stats[:, SSUM:SSUM + TI],
                -INV_D, None, ALU.mult), inc=True)
            wait(S, "SV", m_neg)
            for ti in range(TI):
                wait(S, "SS", last_dump[0])
                m = step(S, lambda e, ti=ti: e.activation(
                    dump[:, :], src_tm[:, ti, :], AF.Square,
                    bias=stats[:, SNEG + ti:SNEG + ti + 1],
                    accum_out=stats[:, SSQD + ti:SSQD + ti + 1]), inc=True)
                last_dump[0] = m
            wait(V, "SS", m)
            m_veps = step(V, lambda e: e.tensor_scalar(
                stats[:, SVEPS:SVEPS + TI], stats[:, SSQD:SSQD + TI],
                INV_D, EPS, ALU.mult, ALU.add), inc=True)
            wait(S, "SV", m_veps)
            m_std = step(S, lambda e: e.activation(
                stats[:, SSTD:SSTD + TI], stats[:, SVEPS:SVEPS + TI],
                AF.Sqrt), inc=True)
            wait(V, "SS", m_std)
            m_rstd = step(V, lambda e: e.reciprocal(
                stats[:, SRSTD:SRSTD + TI], stats[:, SSTD:SSTD + TI]), inc=True)
            return m_rstd

        def ln_apply_to(dst_tm, src_tm):
            for ti in range(TI):
                m = step(V, lambda e, ti=ti: e.tensor_scalar(
                    dst_tm[:, ti, :], src_tm[:, ti, :],
                    stats[:, SNEG + ti:SNEG + ti + 1],
                    stats[:, SRSTD + ti:SRSTD + ti + 1],
                    ALU.add, ALU.mult), inc=(ti == TI - 1))
            return m

        def ln_apply_residual(src_tm, y_view):
            for ti in range(TI):
                wait(V, "SV", last_tmp[0])
                step(V, lambda e, ti=ti: e.tensor_scalar(
                    tmp_tm[:, :], src_tm[:, ti, :],
                    stats[:, SNEG + ti:SNEG + ti + 1],
                    stats[:, SRSTD + ti:SRSTD + ti + 1],
                    ALU.add, ALU.mult))
                m = step(V, lambda e, ti=ti, y_view=y_view: e.tensor_tensor(
                    y_view[:, ti, :], y_view[:, ti, :], tmp_tm[:, :], ALU.add),
                    inc=True)
                last_tmp[0] = m
            return m

        def transpose_tm_to_fm(m_src_ready):
            wait(PE, "SV", m_src_ready)
            m_evac = 0
            for ti in range(TI):
                use(PE, "trA")
                for dc in range(DC):
                    m_tr = step(PE, lambda e, ti=ti, dc=dc: e.transpose(
                        ps_trA[:, dc * 128:(dc + 1) * 128],
                        ln_tm[:, ti, dc * 128:(dc + 1) * 128], idb[:, :]),
                        inc=(dc == DC - 1))
                wait(S, "PE", m_tr)
                m_evac = step(S, lambda e, ti=ti: e.copy(
                    ln_fm[:, :, ti * 128:(ti + 1) * 128],
                    ps_trA[:, :].rearrange("p (a b) -> p a b", a=DC)), inc=True)
                release("trA", "SS", m_evac)
            return m_evac, m_tr

        TRB_POOL = [("bank2", 2), ("bank3", 3), ("bank4", 4), ("bank5", 5),
                    ("trB", None)]

        def transpose_fm_to_tm(m_src_ready_ss):
            wait(PE, "SS", m_src_ready_ss)
            m_evac = 0
            for bi in range(2 * TI):
                ti, hf = divmod(bi, 2)
                rname, bidx = TRB_POOL[bi % len(TRB_POOL)]
                pview = ps_trB[:, :] if bidx is None else ps_mm[:, bidx, :]
                use(PE, rname)
                for j in range(4):
                    oc = hf * 4 + j
                    m_tr = step(PE, lambda e, ti=ti, oc=oc, j=j, pview=pview:
                        e.transpose(
                            pview[:, j * 128:(j + 1) * 128],
                            af_fm[:, oc, ti * 128:(ti + 1) * 128], idf[:, :]),
                        inc=(j == 3))
                wait(V, "PE", m_tr)
                m_evac = step(V, lambda e, ti=ti, hf=hf, pview=pview: e.tensor_copy(
                    af_tm[:, ti, hf * 512:(hf + 1) * 512]
                    .rearrange("p (a b) -> p a b", a=4),
                    pview.rearrange("p (a b) -> p a b", a=4)), inc=True)
                release(rname, "SV", m_evac)
            return m_evac

        def stream_small(q, name, groups, gsz):
            O, K = WSHAPES[name]
            wait_cache(name)
            res = []
            for gi in range(groups):
                s_i = acquire_full(q)
                m = dma(q, wstr[:, s_i, 0:K // 128, 0:gsz * 128],
                        wt_block(name, 0, K,
                                 gi * gsz * 128, (gi + 1) * gsz * 128))
                res.append((s_i, m))
            return res

        m_store = 0
        m_lnT_free = 0
        m_carry_prev = {}
        m_y2 = 0

        def emit_head_ln(tau):
            y_v = y2_tm[:, tau % 2, :, :]
            if tau == 0:
                m_x = m_x0
            else:
                wait(G, "LW", m_store)
                m_x = dma(G, y_v,
                          xin[tau * T:(tau + 1) * T, :]
                          .rearrange("(ti p) d -> p ti d", p=128))
            ln_stats(y_v, [(S, "LX", m_x), (V, "LX", m_x)])
            wait(V, "PE", m_lnT_free)
            return ln_apply_to(ln_tm, y_v)

        def emit_tail(ptau, m_fev_last):
            nonlocal m_store, m_y2
            y_v = y2_tm[:, ptau % 2, :, :]
            m_ftm = transpose_fm_to_tm(m_fev_last)
            ln_stats(af_tm, [(S, "SV", m_ftm), (V, "SV", m_ftm)])
            m_y2 = ln_apply_residual(af_tm, y_v)
            wait(G, "SV", m_y2)
            m_store = dma(G, out[ptau * T:(ptau + 1) * T, :]
                          .rearrange("(ti p) d -> p ti d", p=128),
                          y_v, sem_override="LW")

        head = transpose_tm_to_fm(emit_head_ln(0))

        for tau in range(NT):
            y_cur = y2_tm[:, tau % 2, :, :]
            m_lnfm, m_A2_pe = head

            if tau == 0:
                flush_stage2(["B", "W"])
            blksB = stream_small(SP, "B", 2, 4)
            wait(PE, "SS", m_lnfm)
            m_scan = {}
            m_abc = {}
            m_u = {}
            for nch in range(DC):
                s_i = blksB[nch // 4][0]
                if nch % 4 == 0:
                    wait(PE, "LD", blksB[nch // 4][1])
                bank = nch % 2
                use(PE, f"bank{bank}")
                for kc in range(DC):
                    m_u[nch] = step(PE, lambda e, s_i=s_i, kc=kc, nch=nch, bank=bank:
                        e.matmul(ps_mm[:, bank, :],
                                 wstr[:, s_i, kc, (nch % 4) * 128:(nch % 4 + 1) * 128],
                                 ln_fm[:, kc, :],
                                 start=(kc == 0), stop=(kc == DC - 1)),
                        inc=(kc == DC - 1))
                use(S, f"abc{nch % 4}")
                m_abc[nch] = step(S, lambda e, nch=nch: e.activation(
                    abc[:, nch % 4, :], ones[:, :], AF.Copy,
                    scale=vecs[:, VA + nch:VA + nch + 1]), inc=True)
                wait(V, "PE", m_u[nch])
                wait(V, "SS", m_abc[nch])
                wait(V, "SS", m_carry_prev.get(nch, 0))
                init = 0.0 if tau % TPS == 0 else carry[:, 8 * nch:8 * nch + 1]
                m_scan[nch] = step(V, lambda e, nch=nch, init=init:
                    e.tensor_tensor_scan(
                        h_fm[:, nch, :], abc[:, nch % 4, :], ps_mm[:, nch % 2, :],
                        init, ALU.mult, ALU.add), inc=True)
                release(f"bank{nch % 2}", "SV", m_scan[nch])
                release(f"abc{nch % 4}", "SV", m_scan[nch])
                wait(S, "SV", m_scan[nch])
                m_carry_prev[nch] = step(S, lambda e, nch=nch: e.copy(
                    carry[:, 8 * nch:8 * nch + 1], h_fm[:, nch, T - 1:T]),
                    inc=True)
            for gi in range(2):
                release_full(blksB[gi][0], "PE", m_u[gi * 4 + 3])

            if tau > 0:
                emit_tail(*tail_state)

            if tau == 0:
                flush_stage2(["W"])
            blksW = stream_small(SP, "W", 2, 4)
            m_aev = {}
            for oc in range(DC):
                if oc % 4 == 0:
                    wait(PE, "LD", blksW[oc // 4][1])
                s_i = blksW[oc // 4][0]
                bank = oc % 4
                use(PE, f"bank{bank}")
                for kc in range(DC):
                    if oc == 0:
                        wait(PE, "SV", m_scan[kc])
                    m_a = step(PE, lambda e, s_i=s_i, kc=kc, oc=oc, bank=bank:
                        e.matmul(ps_mm[:, bank, :],
                                 wstr[:, s_i, kc, (oc % 4) * 128:(oc % 4 + 1) * 128],
                                 h_fm[:, kc, :],
                                 start=(kc == 0), stop=(kc == DC - 1)),
                        inc=(kc == DC - 1))
                wait(S, "PE", m_a)
                m_aev[oc] = step(S, lambda e, oc=oc, bank=bank: e.activation(
                    af_fm[:, oc, :], ps_mm[:, bank, :], AF.Identity,
                    bias=vecs[:, VB + oc:VB + oc + 1]), inc=True)
                release(f"bank{bank}", "SS", m_aev[oc])
            for gi in range(2):
                release_full(blksW[gi][0], "PE", m_a)

            m_atm = transpose_fm_to_tm(m_aev[DC - 1])

            ln_stats(af_tm, [(S, "SV", m_atm), (V, "SV", m_atm)])
            m_y1 = ln_apply_residual(af_tm, y_cur)

            ln_stats(y_cur, [(S, "SV", m_y1), (V, "SV", m_y1)])
            wait(V, "PE", m_A2_pe)
            m_lntm = ln_apply_to(ln_tm, y_cur)

            m_lnfm, m_lnT_free = transpose_tm_to_fm(m_lntm)

            if tau == 0:
                flush_stage2(["w1", "w2"])
            wait(PE, "SS", m_lnfm)
            wait_cache("w1")
            wait_cache("w2")
            m_g = {}
            blk3 = {}

            def load_b3_group(grp):
                s1_ = acquire_full(SP)
                mm1 = dma(SP, wstr[:, s1_, :, :],
                          wt_block("w1", 0, D, grp * 512, (grp + 1) * 512))
                s2_ = acquire_full(SP)
                mm2 = dma(SP, wstr[:, s2_, :, :],
                          wt_block("w2", 0, D, grp * 512, (grp + 1) * 512))
                blk3[grp] = (s1_, mm1, s2_, mm2)

            load_b3_group(0)
            load_b3_group(1)
            for hc in range(HCN):
                grp, off = divmod(hc, 4)
                s1_, mm1, s2_, mm2 = blk3[grp]
                if off == 0:
                    wait(PE, "LD", mm2)
                pair = hc % 3
                b1, b2 = 2 * pair, 2 * pair + 1
                use(PE, f"bank{b1}")
                for kc in range(DC):
                    m_t1 = step(PE, lambda e, s1_=s1_, kc=kc, off=off, b1=b1:
                        e.matmul(ps_mm[:, b1, :],
                                 wstr[:, s1_, kc, off * 128:(off + 1) * 128],
                                 ln_fm[:, kc, :],
                                 start=(kc == 0), stop=(kc == DC - 1)),
                        inc=(kc == DC - 1))
                use(PE, f"bank{b2}")
                for kc in range(DC):
                    m_t2 = step(PE, lambda e, s2_=s2_, kc=kc, off=off, b2=b2:
                        e.matmul(ps_mm[:, b2, :],
                                 wstr[:, s2_, kc, off * 128:(off + 1) * 128],
                                 ln_fm[:, kc, :],
                                 start=(kc == 0), stop=(kc == DC - 1)),
                        inc=(kc == DC - 1))
                if off == 3:
                    release_full(s1_, "PE", m_t1)
                    release_full(s2_, "PE", m_t2)
                    if grp + 2 < 8:
                        load_b3_group(grp + 2)
                wait(S, "PE", m_t1)
                use(S, f"silu{hc % 2}")
                m_sl = step(S, lambda e, hc=hc, b1=b1: e.activation(
                    silu[:, hc % 2, :], ps_mm[:, b1, :], AF.Silu,
                    scale=vecs[:, VS1 + hc:VS1 + hc + 1]), inc=True)
                release(f"bank{b1}", "SS", m_sl)
                wait(V, "PE", m_t2)
                wait(V, "SS", m_sl)
                m_g[hc] = step(V, lambda e, hc=hc, b2=b2: e.scalar_tensor_tensor(
                    g_fm[:, hc, :], ps_mm[:, b2, :],
                    vecs[:, VS2 + hc:VS2 + hc + 1], silu[:, hc % 2, :],
                    ALU.mult, ALU.mult), inc=True)
                release(f"bank{b2}", "SV", m_g[hc])
                release(f"silu{hc % 2}", "SV", m_g[hc])

            def big_layer(wname, src_fm, dst_fm, vcol, m_src_ready, sem_src):
                if tau == 0:
                    flush_stage2([wname])
                wait(PE, sem_src, m_src_ready)
                wait_cache(wname)
                m_ev = {}
                for grp in range(16):
                    halves = []
                    for kb in range(4):
                        h_i = acquire_half(SP)
                        m = dma(SP, half_view(h_i),
                                wt_block(wname, kb * 1024, (kb + 1) * 1024,
                                         grp * 256, (grp + 1) * 256))
                        halves.append((h_i, m))
                    pair = grp % 3
                    b1, b2 = 2 * pair, 2 * pair + 1
                    use(PE, f"bank{b1}")
                    use(PE, f"bank{b2}")
                    for kb in range(4):
                        h_i, m = halves[kb]
                        wait(PE, "LD", m)
                        for kc8 in range(8):
                            kc = kb * 8 + kc8
                            for j, bk in ((0, b1), (1, b2)):
                                m_mm = step(PE, lambda e, h_i=h_i, kc8=kc8, j=j,
                                            bk=bk, kc=kc:
                                    e.matmul(ps_mm[:, bk, :],
                                             half_view(h_i)[:, kc8, j * 128:(j + 1) * 128],
                                             src_fm[:, kc, :],
                                             start=(kc == 0), stop=(kc == HCN - 1)),
                                    inc=(kc8 == 7 and j == 1))
                        release(f"half{h_i}", "PE", m_mm)
                    for j, bk in ((0, b1), (1, b2)):
                        hc = grp * 2 + j
                        wait(S, "PE", m_mm)
                        m_ev[hc] = step(S, lambda e, hc=hc, bk=bk, vcol=vcol: e.activation(
                            dst_fm[:, hc, :], ps_mm[:, bk, :], AF.Gelu,
                            scale=vecs[:, vcol + hc:vcol + hc + 1]), inc=True)
                        release(f"bank{bk}", "SS", m_ev[hc])
                return m_ev

            if tau + 1 < NT:
                m_lntm_next = emit_head_ln(tau + 1)
            m_g2 = big_layer("h1", g_fm, g2_fm, VH1, m_g[HCN - 1], "SV")
            if tau + 1 < NT:
                head = transpose_tm_to_fm(m_lntm_next)
            m_g3 = big_layer("h2", g2_fm, g_fm, VH2, m_g2[HCN - 1], "SS")

            if tau == 0:
                flush_stage2(["w3"])
            wait(PE, "SS", m_g3[HCN - 1])
            wait_cache("w3")
            m_fev = {}
            for grp in range(4):
                halves = []
                for kb in range(4):
                    h_i = acquire_half(SP)
                    m = dma(SP, half_view(h_i),
                            wt_block("w3", kb * 1024, (kb + 1) * 1024,
                                     grp * 256, (grp + 1) * 256))
                    halves.append((h_i, m))
                pair = grp % 3
                b1, b2 = 2 * pair, 2 * pair + 1
                use(PE, f"bank{b1}")
                use(PE, f"bank{b2}")
                for kb in range(4):
                    h_i, m = halves[kb]
                    wait(PE, "LD", m)
                    for kc8 in range(8):
                        kc = kb * 8 + kc8
                        for j, bk in ((0, b1), (1, b2)):
                            m_mm = step(PE, lambda e, h_i=h_i, kc8=kc8, j=j,
                                        bk=bk, kc=kc:
                                e.matmul(ps_mm[:, bk, :],
                                         half_view(h_i)[:, kc8, j * 128:(j + 1) * 128],
                                         g_fm[:, kc, :],
                                         start=(kc == 0), stop=(kc == HCN - 1)),
                                inc=(kc8 == 7 and j == 1))
                    release(f"half{h_i}", "PE", m_mm)
                for j, bk in ((0, b1), (1, b2)):
                    oc = grp * 2 + j
                    wait(S, "PE", m_mm)
                    m_fev[oc] = step(S, lambda e, oc=oc, bk=bk: e.activation(
                        af_fm[:, oc, :], ps_mm[:, bk, :], AF.Copy,
                        scale=vecs[:, VS3 + oc:VS3 + oc + 1]), inc=True)
                    release(f"bank{bk}", "SS", m_fev[oc])

            tail_state = (tau, m_fev[DC - 1])

        emit_tail(*tail_state)

        wait(G, "LX", cnt["LX"])
        wait(G, "LW", cnt["LW"])
        wait(SP, "LD", cnt["LD"])

        with nc.Block() as block:
            @block.tensor
            def _(t):
                for f in ops[PE]:
                    f(t)

            @block.vector
            def _(v):
                for f in ops[V]:
                    f(v)

            @block.scalar
            def _(s):
                for f in ops[S]:
                    f(s)

            @block.sync
            def _(s):
                with s.register("sp_off") as r:
                    REG[SP] = r
                    for f in ops[SP]:
                        f(s)

            @block.gpsimd
            def _(g):
                with g.register("g_off") as r:
                    REG[G] = r
                    for f in ops[G]:
                        f(g)

    return nc


_CACHE = {}


def kernel(**inputs):
    x = np.ascontiguousarray(np.asarray(inputs["L0_attn_x"], dtype=np.float32))
    vec_parts = [
        np.asarray(inputs["L0_attn_A"], dtype=np.float32).reshape(DC, 128).T,
        np.asarray(inputs["L0_attn_b"], dtype=np.float32).reshape(DC, 128).T,
        np.asarray(inputs["L0_ffn_w3_s"], dtype=np.float32).reshape(DC, 128).T,
        np.asarray(inputs["L0_ffn_w1_s"], dtype=np.float32).reshape(HCN, 128).T,
        np.asarray(inputs["L0_ffn_w2_s"], dtype=np.float32).reshape(HCN, 128).T,
        np.asarray(inputs["L0_ffn_h1_s"], dtype=np.float32).reshape(HCN, 128).T,
        np.asarray(inputs["L0_ffn_h2_s"], dtype=np.float32).reshape(HCN, 128).T,
    ]
    vecs = np.ascontiguousarray(np.concatenate(vec_parts, axis=1))
    identf = np.eye(128, dtype=np.float32)
    identb = np.eye(128).astype(ml_dtypes.bfloat16)
    common = {"vecs": vecs, "identf": identf, "identb": identb}
    for n in WSHAPES:
        common["w_" + n] = np.ascontiguousarray(
            np.asarray(inputs[WSRC[n]], dtype=np.float32))

    in_maps = []
    for c in range(NCORES):
        m = dict(common)
        m["xin"] = np.ascontiguousarray(
            x[c * BLOC:(c + 1) * BLOC].reshape(TOK, D))
        m["role"] = np.array([[c % 2]], dtype=np.int32)
        in_maps.append(m)

    nc = build()
    res = run_bass_kernel_spmd(nc, in_maps, core_ids=list(range(NCORES)))
    out = np.empty((BATCH, SEQ, D), dtype=np.float32)
    for c in range(NCORES):
        out[c * BLOC:(c + 1) * BLOC] = res.results[c]["out"].reshape(BLOC, SEQ, D)
    return out


# version 20
# speedup vs baseline: 1.1110x; 1.1110x over previous
import numpy as np
import ml_dtypes
import concourse.bass as bass
import concourse.mybir as mybir
from concourse.bass_utils import run_bass_kernel_spmd

F32 = mybir.dt.float32
BF16 = mybir.dt.bfloat16
AF = mybir.ActivationFunctionType
ALU = mybir.AluOpType

D = 1024
H = 4096
SEQ = 2048
BATCH = 16
NCORES = 8
BLOC = BATCH // NCORES
TOK = BLOC * SEQ
T = 512
NT = TOK // T
TPS = SEQ // T
TI = T // 128
DC = D // 128
HCN = H // 128
EPS = 1e-5
INV_D = 1.0 / D

WSHAPES = {
    "B": (D, D), "W": (D, D),
    "w1": (H, D), "w2": (H, D),
    "h1": (H, H), "h2": (H, H),
    "w3": (D, H),
}
WSRC = {
    "B": "L0_attn_B", "W": "L0_attn_W",
    "w1": "L0_ffn_w1_w", "w2": "L0_ffn_w2_w",
    "h1": "L0_ffn_h1_w", "h2": "L0_ffn_h2_w",
    "w3": "L0_ffn_w3_w",
}

VA, VB, VS3, VS1, VS2, VH1, VH2 = 0, 8, 16, 24, 56, 88, 120
NVEC = 152

SSUM, SNEG, SSQD, SVEPS, SSTD, SRSTD = 0, 8, 16, 24, 32, 40
NSTAT = 56

PE, V, S, SP, G = "pe", "v", "s", "sp", "g"


def build():
    nc = bass.Bass("TRN2", target_bir_lowering=False)

    xin = nc.dram_tensor("xin", [TOK, D], F32, kind="ExternalInput").ap()
    out = nc.dram_tensor("out", [TOK, D], F32, kind="ExternalOutput").ap()
    w_in = {n: nc.dram_tensor("w_" + n, list(sh), F32, kind="ExternalInput").ap()
            for n, sh in WSHAPES.items()}
    vecs_in = nc.dram_tensor("vecs", [128, NVEC], F32, kind="ExternalInput").ap()
    idf_in = nc.dram_tensor("identf", [128, 128], F32, kind="ExternalInput").ap()
    idb_in = nc.dram_tensor("identb", [128, 128], BF16, kind="ExternalInput").ap()

    w_plain = {n: nc.dram_tensor("p_" + n, list(sh), BF16, kind="Internal").ap()
               for n, sh in WSHAPES.items()}
    w_t = {n: nc.dram_tensor("t_" + n, [sh[1], sh[0]], BF16, kind="Internal").ap()
           for n, sh in WSHAPES.items()}

    from contextlib import ExitStack
    with ExitStack() as ctx:
        def sb(name, shape, dt):
            return ctx.enter_context(nc.sbuf_tensor(name, shape, dt))

        def ps(name, shape, dt):
            return ctx.enter_context(nc.psum_tensor(name, shape, dt))

        y2_tm = sb("y2_tm", [128, 2, TI, D], F32)
        ln_tm = sb("ln_tm", [128, TI, D], BF16)
        ln_fm = sb("ln_fm", [128, DC, T], BF16)
        h_fm = sb("h_fm", [128, DC, T], BF16)
        af_fm = sb("af_fm", [128, DC, T], F32)
        af_tm = sb("af_tm", [128, TI, D], F32)
        g_fm = sb("g_fm", [128, HCN, T], BF16)
        g2_fm = sb("g2_fm", [128, HCN, T], BF16)
        dump = sb("dump", [128, D], F32)
        tmp_tm = sb("tmp_tm", [128, D], F32)
        stats = sb("stats", [128, NSTAT], F32)
        abc = sb("abc", [128, 4, T], F32)
        silu = sb("silu", [128, 2, T], F32)
        carry = sb("carry", [128, 8 * DC], F32)
        ones = sb("ones", [128, T], F32)
        vecs = sb("vecs_sb", [128, NVEC], F32)
        idf = sb("idf", [128, 128], F32)
        idb = sb("idb", [128, 128], BF16)
        wstr = sb("wstr", [128, 4, 8, 512], BF16)

        ps_mm = ps("ps_mm", [128, 6, 512], F32)
        ps_trA = ps("ps_trA", [128, 1024], BF16)
        ps_trB = ps("ps_trB", [128, 512], F32)

        sem = {k: ctx.enter_context(nc.semaphore("sem_" + k))
               for k in ["LD", "LX", "LW", "PE", "SV", "SS"]}
        cnt = {k: 0 for k in sem}

        ops = {q: [] for q in (PE, V, S, SP, G)}
        ENG_SEM = {PE: "PE", V: "SV", S: "SS", G: "LX"}

        def emit(q, f):
            ops[q].append(f)

        def wait(q, semname, val):
            if val <= 0:
                return
            s_h = sem[semname]
            emit(q, lambda e, s_h=s_h, val=val: e.wait_ge(s_h, val))

        def dma(q, dst, src, sem_override=None, **kw):
            semname = sem_override or ("LD" if q == SP else "LX")
            cnt[semname] += 16
            tgt = cnt[semname]
            s_h = sem[semname]
            emit(q, lambda e, dst=dst, src=src, s_h=s_h, kw=kw:
                 e.dma_start(dst, src, **kw).then_inc(s_h, 16))
            return tgt

        def step(q, fn, inc=False):
            if inc:
                semname = ENG_SEM[q]
                cnt[semname] += 1
                tgt = cnt[semname]
                s_h = sem[semname]
                emit(q, lambda e, fn=fn, s_h=s_h: fn(e).then_inc(s_h, 1))
                return tgt
            emit(q, lambda e, fn=fn: fn(e))
            return None

        last_dump = [0]
        last_tmp = [0]

        rel = {}

        def use(q, name):
            r = rel.get(name)
            if r:
                wait(q, r[0], r[1])

        def release(name, semname, val):
            rel[name] = (semname, val)

        dma(G, vecs[:, :], vecs_in)
        dma(G, idf[:, :], idf_in)
        dma(G, idb[:, :], idb_in)
        m_consts = cnt["LX"]
        m_ones = step(V, lambda e: e.memset(ones[:, :], 1.0), inc=True)
        wait(S, "SV", m_ones)
        wait(PE, "LX", m_consts)
        wait(S, "LX", m_consts)
        wait(V, "LX", m_consts)

        m_x0 = dma(G, y2_tm[:, 0, :, :],
                   xin[0:T, :].rearrange("(ti p) d -> p ti d", p=128))

        m_plain = {}
        for n, (O, K) in WSHAPES.items():
            CH = 1024
            for r0 in range(0, O, CH):
                m = dma(G, w_plain[n][r0:r0 + CH, :], w_in[n][r0:r0 + CH, :])
            m_plain[n] = m

        slot_rr = [0]

        def acquire_full(q):
            s_i = slot_rr[0] % 4
            slot_rr[0] += 1
            use(q, f"half{2 * s_i}")
            use(q, f"half{2 * s_i + 1}")
            return s_i

        def release_full(s_i, semname, val):
            release(f"half{2 * s_i}", semname, val)
            release(f"half{2 * s_i + 1}", semname, val)

        half_rr = [0]

        def acquire_half(q):
            h_i = half_rr[0] % 8
            half_rr[0] += 1
            use(q, f"half{h_i}")
            return h_i

        def half_view(h_i):
            s_i, j = divmod(h_i, 2)
            return wstr[:, s_i, :, j * 256:(j + 1) * 256]

        stage2_done = {}

        def flush_stage2(names):
            for n in names:
                if n in stage2_done:
                    continue
                O, K = WSHAPES[n]
                wait(SP, "LX", m_plain[n])
                for kc in range(K // 128):
                    s_i = acquire_full(SP)
                    tb = wstr[:, s_i, :, :].rearrange("p a b -> p (a b)")[:, 0:O]
                    m1 = dma(SP, tb, w_plain[n][0:O, kc * 128:(kc + 1) * 128],
                             transpose=True)
                    wait(G, "LD", m1)
                    m2 = dma(G, w_t[n][kc * 128:(kc + 1) * 128, 0:O], tb,
                             sem_override="LW")
                    release_full(s_i, "LW", m2)
                stage2_done[n] = cnt["LW"]

        def wait_cache(n):
            wait(SP, "LW", stage2_done[n])


        def ln_stats(src_tm, wait_list):
            for q, semname, val in wait_list:
                wait(q, semname, val)
            for ti in range(TI):
                wait(S, "SS", last_dump[0])
                m = step(S, lambda e, ti=ti: e.activation(
                    dump[:, :], src_tm[:, ti, :], AF.Copy,
                    accum_out=stats[:, SSUM + ti:SSUM + ti + 1]), inc=True)
                last_dump[0] = m
            wait(V, "SS", m)
            m_neg = step(V, lambda e: e.tensor_scalar(
                stats[:, SNEG:SNEG + TI], stats[:, SSUM:SSUM + TI],
                -INV_D, None, ALU.mult), inc=True)
            wait(S, "SV", m_neg)
            for ti in range(TI):
                wait(S, "SS", last_dump[0])
                m = step(S, lambda e, ti=ti: e.activation(
                    dump[:, :], src_tm[:, ti, :], AF.Square,
                    bias=stats[:, SNEG + ti:SNEG + ti + 1],
                    accum_out=stats[:, SSQD + ti:SSQD + ti + 1]), inc=True)
                last_dump[0] = m
            wait(V, "SS", m)
            m_veps = step(V, lambda e: e.tensor_scalar(
                stats[:, SVEPS:SVEPS + TI], stats[:, SSQD:SSQD + TI],
                INV_D, EPS, ALU.mult, ALU.add), inc=True)
            wait(S, "SV", m_veps)
            m_std = step(S, lambda e: e.activation(
                stats[:, SSTD:SSTD + TI], stats[:, SVEPS:SVEPS + TI],
                AF.Sqrt), inc=True)
            wait(V, "SS", m_std)
            m_rstd = step(V, lambda e: e.reciprocal(
                stats[:, SRSTD:SRSTD + TI], stats[:, SSTD:SSTD + TI]), inc=True)
            return m_rstd

        def ln_apply_to(dst_tm, src_tm):
            for ti in range(TI):
                m = step(V, lambda e, ti=ti: e.tensor_scalar(
                    dst_tm[:, ti, :], src_tm[:, ti, :],
                    stats[:, SNEG + ti:SNEG + ti + 1],
                    stats[:, SRSTD + ti:SRSTD + ti + 1],
                    ALU.add, ALU.mult), inc=(ti == TI - 1))
            return m

        def ln_apply_residual(src_tm, y_view):
            for ti in range(TI):
                wait(V, "SV", last_tmp[0])
                step(V, lambda e, ti=ti: e.tensor_scalar(
                    tmp_tm[:, :], src_tm[:, ti, :],
                    stats[:, SNEG + ti:SNEG + ti + 1],
                    stats[:, SRSTD + ti:SRSTD + ti + 1],
                    ALU.add, ALU.mult))
                m = step(V, lambda e, ti=ti, y_view=y_view: e.tensor_tensor(
                    y_view[:, ti, :], y_view[:, ti, :], tmp_tm[:, :], ALU.add),
                    inc=True)
                last_tmp[0] = m
            return m

        def ln_stats_half(src_tm, h, wait_list):
            for q, semname, val in wait_list:
                wait(q, semname, val)
            for ti in (2 * h, 2 * h + 1):
                wait(S, "SS", last_dump[0])
                m = step(S, lambda e, ti=ti: e.activation(
                    dump[:, :], src_tm[:, ti, :], AF.Copy,
                    accum_out=stats[:, SSUM + ti:SSUM + ti + 1]), inc=True)
                last_dump[0] = m
            wait(V, "SS", m)
            m_neg = step(V, lambda e, h=h: e.tensor_scalar(
                stats[:, SNEG + 2 * h:SNEG + 2 * h + 2],
                stats[:, SSUM + 2 * h:SSUM + 2 * h + 2],
                -INV_D, None, ALU.mult), inc=True)
            wait(S, "SV", m_neg)
            for ti in (2 * h, 2 * h + 1):
                wait(S, "SS", last_dump[0])
                m = step(S, lambda e, ti=ti: e.activation(
                    dump[:, :], src_tm[:, ti, :], AF.Square,
                    bias=stats[:, SNEG + ti:SNEG + ti + 1],
                    accum_out=stats[:, SSQD + ti:SSQD + ti + 1]), inc=True)
                last_dump[0] = m
            wait(V, "SS", m)
            m_veps = step(V, lambda e, h=h: e.tensor_scalar(
                stats[:, SVEPS + 2 * h:SVEPS + 2 * h + 2],
                stats[:, SSQD + 2 * h:SSQD + 2 * h + 2],
                INV_D, EPS, ALU.mult, ALU.add), inc=True)
            wait(S, "SV", m_veps)
            m_std = step(S, lambda e, h=h: e.activation(
                stats[:, SSTD + 2 * h:SSTD + 2 * h + 2],
                stats[:, SVEPS + 2 * h:SVEPS + 2 * h + 2],
                AF.Sqrt), inc=True)
            wait(V, "SS", m_std)
            m_rstd = step(V, lambda e, h=h: e.reciprocal(
                stats[:, SRSTD + 2 * h:SRSTD + 2 * h + 2],
                stats[:, SSTD + 2 * h:SSTD + 2 * h + 2]), inc=True)
            return m_rstd

        def ln_apply_to_half(dst_tm, src_tm, h):
            for ti in (2 * h, 2 * h + 1):
                m = step(V, lambda e, ti=ti: e.tensor_scalar(
                    dst_tm[:, ti, :], src_tm[:, ti, :],
                    stats[:, SNEG + ti:SNEG + ti + 1],
                    stats[:, SRSTD + ti:SRSTD + ti + 1],
                    ALU.add, ALU.mult), inc=(ti == 2 * h + 1))
            return m

        def ln_apply_residual_half(src_tm, y_view, h):
            for ti in (2 * h, 2 * h + 1):
                wait(V, "SV", last_tmp[0])
                step(V, lambda e, ti=ti: e.tensor_scalar(
                    tmp_tm[:, :], src_tm[:, ti, :],
                    stats[:, SNEG + ti:SNEG + ti + 1],
                    stats[:, SRSTD + ti:SRSTD + ti + 1],
                    ALU.add, ALU.mult))
                m = step(V, lambda e, ti=ti, y_view=y_view: e.tensor_tensor(
                    y_view[:, ti, :], y_view[:, ti, :], tmp_tm[:, :], ALU.add),
                    inc=True)
                last_tmp[0] = m
            return m

        def transpose_tm_to_fm_half(m_src_ready, h):
            wait(PE, "SV", m_src_ready)
            m_evac = 0
            for ti in (2 * h, 2 * h + 1):
                use(PE, "trA")
                for dc in range(DC):
                    m_tr = step(PE, lambda e, ti=ti, dc=dc: e.transpose(
                        ps_trA[:, dc * 128:(dc + 1) * 128],
                        ln_tm[:, ti, dc * 128:(dc + 1) * 128], idb[:, :]),
                        inc=(dc == DC - 1))
                wait(S, "PE", m_tr)
                m_evac = step(S, lambda e, ti=ti: e.copy(
                    ln_fm[:, :, ti * 128:(ti + 1) * 128],
                    ps_trA[:, :].rearrange("p (a b) -> p a b", a=DC)), inc=True)
                release("trA", "SS", m_evac)
            return m_evac, m_tr

        def transpose_tm_to_fm(m_src_ready):
            wait(PE, "SV", m_src_ready)
            m_evac = 0
            for ti in range(TI):
                use(PE, "trA")
                for dc in range(DC):
                    m_tr = step(PE, lambda e, ti=ti, dc=dc: e.transpose(
                        ps_trA[:, dc * 128:(dc + 1) * 128],
                        ln_tm[:, ti, dc * 128:(dc + 1) * 128], idb[:, :]),
                        inc=(dc == DC - 1))
                wait(S, "PE", m_tr)
                m_evac = step(S, lambda e, ti=ti: e.copy(
                    ln_fm[:, :, ti * 128:(ti + 1) * 128],
                    ps_trA[:, :].rearrange("p (a b) -> p a b", a=DC)), inc=True)
                release("trA", "SS", m_evac)
            return m_evac, m_tr

        TRB_POOL = [("bank2", 2), ("bank3", 3), ("bank4", 4), ("bank5", 5),
                    ("trB", None)]

        def transpose_fm_to_tm(m_src_ready_ss):
            wait(PE, "SS", m_src_ready_ss)
            m_evac = 0
            m_list = []
            for bi in range(2 * TI):
                ti, hf = divmod(bi, 2)
                rname, bidx = TRB_POOL[bi % len(TRB_POOL)]
                pview = ps_trB[:, :] if bidx is None else ps_mm[:, bidx, :]
                use(PE, rname)
                for j in range(4):
                    oc = hf * 4 + j
                    m_tr = step(PE, lambda e, ti=ti, oc=oc, j=j, pview=pview:
                        e.transpose(
                            pview[:, j * 128:(j + 1) * 128],
                            af_fm[:, oc, ti * 128:(ti + 1) * 128], idf[:, :]),
                        inc=(j == 3))
                wait(V, "PE", m_tr)
                m_evac = step(V, lambda e, ti=ti, hf=hf, pview=pview: e.tensor_copy(
                    af_tm[:, ti, hf * 512:(hf + 1) * 512]
                    .rearrange("p (a b) -> p a b", a=4),
                    pview.rearrange("p (a b) -> p a b", a=4)), inc=True)
                release(rname, "SV", m_evac)
                m_list.append(m_evac)
            return m_evac, m_list

        def stream_small(q, name, groups, gsz):
            O, K = WSHAPES[name]
            wait_cache(name)
            res = []
            for gi in range(groups):
                s_i = acquire_full(q)
                m = dma(q, wstr[:, s_i, 0:K // 128, 0:gsz * 128],
                        w_t[name][0:K, gi * gsz * 128:(gi + 1) * gsz * 128]
                        .rearrange("(kc p) o -> p kc o", p=128))
                res.append((s_i, m))
            return res

        m_store = 0
        m_lnT_free = 0
        m_carry_prev = {}
        m_y2 = 0

        def emit_head_ln(tau):
            y_v = y2_tm[:, tau % 2, :, :]
            if tau == 0:
                m_x = m_x0
            else:
                wait(G, "LW", m_store)
                m_x = dma(G, y_v,
                          xin[tau * T:(tau + 1) * T, :]
                          .rearrange("(ti p) d -> p ti d", p=128))
            ln_stats(y_v, [(S, "LX", m_x), (V, "LX", m_x)])
            wait(V, "PE", m_lnT_free)
            return ln_apply_to(ln_tm, y_v)

        def emit_tail(ptau, m_fev_last):
            nonlocal m_store, m_y2
            y_v = y2_tm[:, ptau % 2, :, :]
            m_ftm, _ = transpose_fm_to_tm(m_fev_last)
            ln_stats(af_tm, [(S, "SV", m_ftm), (V, "SV", m_ftm)])
            m_y2 = ln_apply_residual(af_tm, y_v)
            wait(G, "SV", m_y2)
            m_store = dma(G, out[ptau * T:(ptau + 1) * T, :]
                          .rearrange("(ti p) d -> p ti d", p=128),
                          y_v, sem_override="LW")

        head = transpose_tm_to_fm(emit_head_ln(0))

        for tau in range(NT):
            y_cur = y2_tm[:, tau % 2, :, :]
            m_lnfm, m_A2_pe = head

            if tau == 0:
                flush_stage2(["B"])
            blksB = stream_small(SP, "B", 2, 4)
            wait(PE, "SS", m_lnfm)
            m_scan = {}
            m_abc = {}
            m_u = {}
            for nch in range(DC):
                s_i = blksB[nch // 4][0]
                if nch % 4 == 0:
                    wait(PE, "LD", blksB[nch // 4][1])
                bank = nch % 2
                use(PE, f"bank{bank}")
                for kc in range(DC):
                    m_u[nch] = step(PE, lambda e, s_i=s_i, kc=kc, nch=nch, bank=bank:
                        e.matmul(ps_mm[:, bank, :],
                                 wstr[:, s_i, kc, (nch % 4) * 128:(nch % 4 + 1) * 128],
                                 ln_fm[:, kc, :],
                                 start=(kc == 0), stop=(kc == DC - 1)),
                        inc=(kc == DC - 1))
                use(S, f"abc{nch % 4}")
                m_abc[nch] = step(S, lambda e, nch=nch: e.activation(
                    abc[:, nch % 4, :], ones[:, :], AF.Copy,
                    scale=vecs[:, VA + nch:VA + nch + 1]), inc=True)
                wait(V, "PE", m_u[nch])
                wait(V, "SS", m_abc[nch])
                wait(V, "SS", m_carry_prev.get(nch, 0))
                init = 0.0 if tau % TPS == 0 else carry[:, 8 * nch:8 * nch + 1]
                m_scan[nch] = step(V, lambda e, nch=nch, init=init:
                    e.tensor_tensor_scan(
                        h_fm[:, nch, :], abc[:, nch % 4, :], ps_mm[:, nch % 2, :],
                        init, ALU.mult, ALU.add), inc=True)
                release(f"bank{nch % 2}", "SV", m_scan[nch])
                release(f"abc{nch % 4}", "SV", m_scan[nch])
                wait(S, "SV", m_scan[nch])
                m_carry_prev[nch] = step(S, lambda e, nch=nch: e.copy(
                    carry[:, 8 * nch:8 * nch + 1], h_fm[:, nch, T - 1:T]),
                    inc=True)
            for gi in range(2):
                release_full(blksB[gi][0], "PE", m_u[gi * 4 + 3])

            if tau > 0:
                emit_tail(*tail_state)

            if tau == 0:
                flush_stage2(["W"])
            blksW = stream_small(SP, "W", 2, 4)
            m_aev = {}
            for oc in range(DC):
                if oc % 4 == 0:
                    wait(PE, "LD", blksW[oc // 4][1])
                s_i = blksW[oc // 4][0]
                bank = oc % 4
                use(PE, f"bank{bank}")
                for kc in range(DC):
                    if oc == 0:
                        wait(PE, "SV", m_scan[kc])
                    m_a = step(PE, lambda e, s_i=s_i, kc=kc, oc=oc, bank=bank:
                        e.matmul(ps_mm[:, bank, :],
                                 wstr[:, s_i, kc, (oc % 4) * 128:(oc % 4 + 1) * 128],
                                 h_fm[:, kc, :],
                                 start=(kc == 0), stop=(kc == DC - 1)),
                        inc=(kc == DC - 1))
                wait(S, "PE", m_a)
                m_aev[oc] = step(S, lambda e, oc=oc, bank=bank: e.activation(
                    af_fm[:, oc, :], ps_mm[:, bank, :], AF.Identity,
                    bias=vecs[:, VB + oc:VB + oc + 1]), inc=True)
                release(f"bank{bank}", "SS", m_aev[oc])
            for gi in range(2):
                release_full(blksW[gi][0], "PE", m_a)

            m_atm, m_atm_list = transpose_fm_to_tm(m_aev[DC - 1])

            m_lnfm_h = []
            for hh in (0, 1):
                m_half = m_atm_list[4 * hh + 3]
                ln_stats_half(af_tm, hh, [(S, "SV", m_half), (V, "SV", m_half)])
                m_y1 = ln_apply_residual_half(af_tm, y_cur, hh)
                ln_stats_half(y_cur, hh, [(S, "SV", m_y1), (V, "SV", m_y1)])
                if hh == 0:
                    wait(V, "PE", m_A2_pe)
                m_lntm_h = ln_apply_to_half(ln_tm, y_cur, hh)
                m_ev_h, m_pe_h = transpose_tm_to_fm_half(m_lntm_h, hh)
                m_lnfm_h.append(m_ev_h)
            m_lnfm = m_lnfm_h[1]
            m_lnT_free = m_pe_h

            if tau == 0:
                flush_stage2(["w1", "w2"])
            wait_cache("w1")
            wait_cache("w2")
            m_g = {}
            blk3 = {}

            def load_b3_group(grp):
                s1_ = acquire_full(SP)
                mm1 = dma(SP, wstr[:, s1_, :, :],
                          w_t["w1"][0:D, grp * 512:(grp + 1) * 512]
                          .rearrange("(kc p) o -> p kc o", p=128))
                s2_ = acquire_full(SP)
                mm2 = dma(SP, wstr[:, s2_, :, :],
                          w_t["w2"][0:D, grp * 512:(grp + 1) * 512]
                          .rearrange("(kc p) o -> p kc o", p=128))
                blk3[grp] = (s1_, mm1, s2_, mm2)

            load_b3_group(0)
            load_b3_group(1)
            for hc in range(HCN):
                grp, off = divmod(hc, 4)
                s1_, mm1, s2_, mm2 = blk3[grp]
                if off == 0:
                    wait(PE, "LD", mm2)
                pair = hc % 3
                b1, b2 = 2 * pair, 2 * pair + 1
                use(PE, f"bank{b1}")
                use(PE, f"bank{b2}")
                for hh in (0, 1):
                    lo, hi = hh * 256, (hh + 1) * 256
                    if hc == 0:
                        wait(PE, "SS", m_lnfm_h[hh])
                    for kc in range(DC):
                        m_t1 = step(PE, lambda e, s1_=s1_, kc=kc, off=off,
                                    b1=b1, lo=lo, hi=hi:
                            e.matmul(ps_mm[:, b1, lo:hi],
                                     wstr[:, s1_, kc, off * 128:(off + 1) * 128],
                                     ln_fm[:, kc, lo:hi],
                                     start=(kc == 0), stop=(kc == DC - 1)),
                            inc=(kc == DC - 1 and hh == 1))
                    for kc in range(DC):
                        m_t2 = step(PE, lambda e, s2_=s2_, kc=kc, off=off,
                                    b2=b2, lo=lo, hi=hi:
                            e.matmul(ps_mm[:, b2, lo:hi],
                                     wstr[:, s2_, kc, off * 128:(off + 1) * 128],
                                     ln_fm[:, kc, lo:hi],
                                     start=(kc == 0), stop=(kc == DC - 1)),
                            inc=(kc == DC - 1 and hh == 1))
                if off == 3:
                    release_full(s1_, "PE", m_t1)
                    release_full(s2_, "PE", m_t2)
                    if grp + 2 < 8:
                        load_b3_group(grp + 2)
                wait(S, "PE", m_t1)
                use(S, f"silu{hc % 2}")
                m_sl = step(S, lambda e, hc=hc, b1=b1: e.activation(
                    silu[:, hc % 2, :], ps_mm[:, b1, :], AF.Silu,
                    scale=vecs[:, VS1 + hc:VS1 + hc + 1]), inc=True)
                release(f"bank{b1}", "SS", m_sl)
                wait(V, "PE", m_t2)
                wait(V, "SS", m_sl)
                m_g[hc] = step(V, lambda e, hc=hc, b2=b2: e.scalar_tensor_tensor(
                    g_fm[:, hc, :], ps_mm[:, b2, :],
                    vecs[:, VS2 + hc:VS2 + hc + 1], silu[:, hc % 2, :],
                    ALU.mult, ALU.mult), inc=True)
                release(f"bank{b2}", "SV", m_g[hc])
                release(f"silu{hc % 2}", "SV", m_g[hc])

            def big_layer(wname, src_fm, dst_fm, vcol, m_src_ready, sem_src):
                if tau == 0:
                    flush_stage2([wname])
                wait(PE, sem_src, m_src_ready)
                wait_cache(wname)
                m_ev = {}
                for grp in range(16):
                    halves = []
                    for kb in range(4):
                        h_i = acquire_half(SP)
                        m = dma(SP, half_view(h_i),
                                w_t[wname][kb * 1024:(kb + 1) * 1024,
                                           grp * 256:(grp + 1) * 256]
                                .rearrange("(kc p) o -> p kc o", p=128))
                        halves.append((h_i, m))
                    pair = grp % 3
                    b1, b2 = 2 * pair, 2 * pair + 1
                    use(PE, f"bank{b1}")
                    use(PE, f"bank{b2}")
                    for kb in range(4):
                        h_i, m = halves[kb]
                        wait(PE, "LD", m)
                        for kc8 in range(8):
                            kc = kb * 8 + kc8
                            for j, bk in ((0, b1), (1, b2)):
                                m_mm = step(PE, lambda e, h_i=h_i, kc8=kc8, j=j,
                                            bk=bk, kc=kc:
                                    e.matmul(ps_mm[:, bk, :],
                                             half_view(h_i)[:, kc8, j * 128:(j + 1) * 128],
                                             src_fm[:, kc, :],
                                             start=(kc == 0), stop=(kc == HCN - 1)),
                                    inc=(kc8 == 7 and j == 1))
                        release(f"half{h_i}", "PE", m_mm)
                    for j, bk in ((0, b1), (1, b2)):
                        hc = grp * 2 + j
                        wait(S, "PE", m_mm)
                        m_ev[hc] = step(S, lambda e, hc=hc, bk=bk, vcol=vcol: e.activation(
                            dst_fm[:, hc, :], ps_mm[:, bk, :], AF.Gelu,
                            scale=vecs[:, vcol + hc:vcol + hc + 1]), inc=True)
                        release(f"bank{bk}", "SS", m_ev[hc])
                return m_ev

            if tau + 1 < NT:
                m_lntm_next = emit_head_ln(tau + 1)
            m_g2 = big_layer("h1", g_fm, g2_fm, VH1, m_g[HCN - 1], "SV")
            if tau + 1 < NT:
                head = transpose_tm_to_fm(m_lntm_next)
            m_g3 = big_layer("h2", g2_fm, g_fm, VH2, m_g2[HCN - 1], "SS")

            if tau == 0:
                flush_stage2(["w3"])
            wait(PE, "SS", m_g3[HCN - 1])
            wait_cache("w3")
            m_fev = {}
            for grp in range(4):
                halves = []
                for kb in range(4):
                    h_i = acquire_half(SP)
                    m = dma(SP, half_view(h_i),
                            w_t["w3"][kb * 1024:(kb + 1) * 1024,
                                      grp * 256:(grp + 1) * 256]
                            .rearrange("(kc p) o -> p kc o", p=128))
                    halves.append((h_i, m))
                pair = grp % 3
                b1, b2 = 2 * pair, 2 * pair + 1
                use(PE, f"bank{b1}")
                use(PE, f"bank{b2}")
                for kb in range(4):
                    h_i, m = halves[kb]
                    wait(PE, "LD", m)
                    for kc8 in range(8):
                        kc = kb * 8 + kc8
                        for j, bk in ((0, b1), (1, b2)):
                            m_mm = step(PE, lambda e, h_i=h_i, kc8=kc8, j=j,
                                        bk=bk, kc=kc:
                                e.matmul(ps_mm[:, bk, :],
                                         half_view(h_i)[:, kc8, j * 128:(j + 1) * 128],
                                         g_fm[:, kc, :],
                                         start=(kc == 0), stop=(kc == HCN - 1)),
                                inc=(kc8 == 7 and j == 1))
                    release(f"half{h_i}", "PE", m_mm)
                for j, bk in ((0, b1), (1, b2)):
                    oc = grp * 2 + j
                    wait(S, "PE", m_mm)
                    m_fev[oc] = step(S, lambda e, oc=oc, bk=bk: e.activation(
                        af_fm[:, oc, :], ps_mm[:, bk, :], AF.Copy,
                        scale=vecs[:, VS3 + oc:VS3 + oc + 1]), inc=True)
                    release(f"bank{bk}", "SS", m_fev[oc])

            tail_state = (tau, m_fev[DC - 1])

        emit_tail(*tail_state)

        wait(G, "LX", cnt["LX"])
        wait(G, "LW", cnt["LW"])
        wait(SP, "LD", cnt["LD"])

        with nc.Block() as block:
            @block.tensor
            def _(t):
                for f in ops[PE]:
                    f(t)

            @block.vector
            def _(v):
                for f in ops[V]:
                    f(v)

            @block.scalar
            def _(s):
                for f in ops[S]:
                    f(s)

            @block.sync
            def _(s):
                for f in ops[SP]:
                    f(s)

            @block.gpsimd
            def _(g):
                for f in ops[G]:
                    f(g)

    return nc


_CACHE = {}


def kernel(**inputs):
    x = np.ascontiguousarray(np.asarray(inputs["L0_attn_x"], dtype=np.float32))
    vec_parts = [
        np.asarray(inputs["L0_attn_A"], dtype=np.float32).reshape(DC, 128).T,
        np.asarray(inputs["L0_attn_b"], dtype=np.float32).reshape(DC, 128).T,
        np.asarray(inputs["L0_ffn_w3_s"], dtype=np.float32).reshape(DC, 128).T,
        np.asarray(inputs["L0_ffn_w1_s"], dtype=np.float32).reshape(HCN, 128).T,
        np.asarray(inputs["L0_ffn_w2_s"], dtype=np.float32).reshape(HCN, 128).T,
        np.asarray(inputs["L0_ffn_h1_s"], dtype=np.float32).reshape(HCN, 128).T,
        np.asarray(inputs["L0_ffn_h2_s"], dtype=np.float32).reshape(HCN, 128).T,
    ]
    vecs = np.ascontiguousarray(np.concatenate(vec_parts, axis=1))
    identf = np.eye(128, dtype=np.float32)
    identb = np.eye(128).astype(ml_dtypes.bfloat16)
    common = {"vecs": vecs, "identf": identf, "identb": identb}
    for n in WSHAPES:
        common["w_" + n] = np.ascontiguousarray(
            np.asarray(inputs[WSRC[n]], dtype=np.float32))

    in_maps = []
    for c in range(NCORES):
        m = dict(common)
        m["xin"] = np.ascontiguousarray(
            x[c * BLOC:(c + 1) * BLOC].reshape(TOK, D))
        in_maps.append(m)

    nc = build()
    res = run_bass_kernel_spmd(nc, in_maps, core_ids=list(range(NCORES)))
    out = np.empty((BATCH, SEQ, D), dtype=np.float32)
    for c in range(NCORES):
        out[c * BLOC:(c + 1) * BLOC] = res.results[c]["out"].reshape(BLOC, SEQ, D)
    return out
